# version 04
# baseline (speedup 1.0000x reference)
import math
from contextlib import ExitStack

import numpy as np

import concourse.bass as bass
import concourse.mybir as mybir
from concourse.bass_utils import run_bass_kernel_spmd

F32 = mybir.dt.float32
BF16 = mybir.dt.bfloat16
AF = mybir.ActivationFunctionType
ALU = mybir.AluOpType

D = 1024
KC = 8
PI = math.pi

FULL_CFG = dict(B=2, L=16384, cps=4, Tg=256)


class Ev(tuple):
    pass


class Buf:

    __slots__ = ("name", "t", "w_ev", "r_evs")

    def __init__(self, name, t):
        self.name = name
        self.t = t
        self.w_ev = None
        self.r_evs = []

    def ap(self):
        return self.t.ap() if hasattr(self.t, "ap") else self.t


class MK:
    ENGS = ("T", "V", "S", "G", "Y")

    def __init__(self, cfg):
        self.cfg = dict(cfg)
        self.B = cfg["B"]
        self.L = cfg["L"]
        self.cps = cfg["cps"]
        self.Tg = cfg["Tg"]
        self.NT = self.Tg // 128
        self.Lc = self.L // self.cps
        self.NG = self.Lc // self.Tg
        assert self.Tg % 128 == 0 and self.Lc % self.Tg == 0
        self.nc = bass.Bass("TRN2", target_bir_lowering=False)
        self.thunks = {e: [] for e in self.ENGS}
        self.ctr = {}
        self.hw = {e: {} for e in self.ENGS}
        self.sem_handles = {}
        self.stack = None
        self.bufs = {}

    def sem(self, name):
        h = self.stack.enter_context(self.nc.semaphore(name))
        self.sem_handles[name] = h
        self.ctr[name] = 0
        return h

    def _engine(self, block, name):
        return {
            "T": block.tensor, "V": block.vector, "S": block.scalar,
            "G": block.gpsimd, "Y": block.sync,
        }[name]

    def emit(self, eng, body, reads=(), writes=(), deps=(), sig=None, n_inc=1):
        sig = sig or eng
        dep_set = {}
        for ev in deps:
            if ev is not None:
                dep_set[ev[0]] = max(dep_set.get(ev[0], 0), ev[1])
        for b in reads:
            if b.w_ev is not None:
                dep_set[b.w_ev[0]] = max(dep_set.get(b.w_ev[0], 0), b.w_ev[1])
        for b in writes:
            for ev in ([b.w_ev] if b.w_ev else []) + b.r_evs:
                dep_set[ev[0]] = max(dep_set.get(ev[0], 0), ev[1])
        waits = []
        hw = self.hw[eng]
        for s, v in dep_set.items():
            if v > hw.get(s, 0):
                waits.append((s, v))
                hw[s] = v
        self.ctr[sig] += n_inc
        my_ev = Ev((sig, self.ctr[sig]))

        def thunk(engine, waits=tuple(waits), body=body, sig=sig, n_inc=n_inc):
            for s, v in waits:
                engine.wait_ge(self.sem_handles[s], v)
            last = body(engine)
            if last is None:
                last = engine.drain()
            last.then_inc(self.sem_handles[sig], n_inc)

        self.thunks[eng].append(thunk)
        for b in reads:
            b.r_evs.append(my_ev)
        for b in writes:
            b.w_ev = my_ev
            b.r_evs = []
        return my_ev

    NQSEM = 8

    def dma(self, q, out_ap, in_ap, reads=(), writes=(), deps=(), accum=None,
            slow=False):
        i = self.qrot[q]
        self.qrot[q] = (i + 1) % self.NQSEM
        sig = f"{q}Q{i}"
        deps = list(deps)
        if self.ctr[sig] > 0:
            deps.append(Ev((sig, self.ctr[sig])))

        def body(engine):
            kw = {}
            if accum is not None:
                kw["accum_op"] = accum
            if slow:
                kw["allow_slow_non_contiguous"] = True
            return engine.dma_start(out=out_ap, in_=in_ap, **kw)

        return self.emit(q, body, reads=reads, writes=writes, deps=deps,
                         sig=sig, n_inc=16)

    def sb(self, name, shape, dtype):
        t = self.stack.enter_context(self.nc.sbuf_tensor(name, shape, dtype))
        b = Buf(name, t)
        self.bufs[name] = b
        return b

    def ps(self, name, shape, dtype=F32):
        t = self.stack.enter_context(self.nc.psum_tensor(name, shape, dtype))
        b = Buf(name, t)
        self.bufs[name] = b
        return b

    def dram(self, name, shape, dtype, kind="Internal"):
        t = self.nc.dram_tensor(name, shape, dtype, kind=kind)
        b = Buf(name, t)
        self.bufs[name] = b
        return b

    def finalize(self, final_deps):
        def fin(engine, deps=tuple(final_deps)):
            for s, v in deps:
                engine.wait_ge(self.sem_handles[s], v)
        with self.nc.Block() as block:
            for e in self.ENGS:
                ths = self.thunks[e]
                if e == "Y":
                    ths = ths + [fin]
                if not ths:
                    continue
                def runner(engine, ths=tuple(ths)):
                    for th in ths:
                        th(engine)
                self._engine(block, e)(runner)


def build(cfg, mode="fused"):
    mk = MK(cfg)
    nc = mk.nc
    NT, Tg, NG, Lc = mk.NT, mk.Tg, mk.NG, mk.Lc
    NTOK = NG * NT

    with ExitStack() as stack:
        mk.stack = stack
        for s in (["T", "V", "S", "G", "CC"]
                  + [f"{q}Q{i}" for q in "YG" for i in range(MK.NQSEM)]):
            mk.sem(s)
        mk.qrot = {"Y": 0, "G": 0}

        di = {}
        for name, shape in [
            ("x", [Lc, D]), ("ph", [Lc, D]),
            ("wpv", [D, D]), ("bpv", [D]), ("wk", [D, D]), ("bk", [D]),
            ("wkv", [D, D]), ("bkv", [D]), ("wg1", [2 * D, D]), ("bg1", [D]),
            ("wg2", [D, 1]), ("bg2", [1]), ("wb1", [D, 512]), ("bb1", [512]),
            ("wb2", [512, 2]), ("bb2", [2]), ("lng", [D]), ("lnb", [D]),
            ("wo", [D, D]), ("bo", [D]), ("wq", [D, D]), ("rout", [255, D]),
            ("lfv", [256, D]), ("wout", [D, D]), ("woutb", [D]),
            ("tri", [128, 128]), ("ident_hbm", [128, 128]),
            ("ones1f_hbm", [1, 128]), ("ones1b_hbm", [1, 128]),
            ("consts_hbm", [128, 8]),
            ("cmask_hbm", [5, cfg["cps"]]), ("xprev", [1, D]),
            ("xprev_fm", [128, KC]), ("bflag", [1, 1]),
        ]:
            dt = BF16 if name in ("ident_hbm", "ones1b_hbm") else F32
            di[name] = mk.dram(name, shape, dt, kind="ExternalInput")
        y = mk.dram("y", [Lc, D], F32, kind="ExternalOutput")

        sp_names = ("a_sp", "b_sp", "u_sp", "v_sp", "kc_sp", "ks_sp",
                    "par_sp", "cs_sp", "sn_sp")
        spkind = ("Internal" if mode == "fused"
                  else "ExternalOutput" if mode == "front" else "ExternalInput")
        spd = {n: mk.dram(n, [Lc, D], F32, kind=spkind) for n in sp_names}
        xnfm_sp = mk.dram("xnfm_sp", [NG * 128, KC * Tg], BF16, kind=spkind)
        cbc_dram = mk.dram("cbc_dram", [4 * 128, D], F32)
        cbc_buf = [Buf(f"cbcb{i}", None) for i in range(4)]
        if mode == "fused":
            cc_in = mk.dram("cc_in", [5, D], F32)
            cc_out = mk.dram("cc_out", [mk.cps * 5, D], F32)
        elif mode == "front":
            cc_in = mk.dram("tot", [5, D], F32, kind="ExternalOutput")
            ga_out = mk.dram("ga_out", [128, NTOK], F32, kind="ExternalOutput")
            da_out = mk.dram("da_out", [128, NTOK], F32, kind="ExternalOutput")
        else:
            cc_out = mk.dram("tots", [mk.cps * 5, D], F32, kind="ExternalInput")
            ga_in = mk.dram("ga_in", [128, NTOK], F32, kind="ExternalInput")
            da_in = mk.dram("da_in", [128, NTOK], F32, kind="ExternalInput")

        def spl(dr, g):
            return dr.ap()[g * Tg:(g + 1) * Tg, :].rearrange(
                "(t p) f -> p t f", p=128)

        def sb3(buf):
            return buf.ap().rearrange("p (t f) -> p t f", f=D)

        tri = mk.sb("tri_s", [128, 128], F32)
        ident = mk.sb("ident_s", [128, 128], BF16)
        ones1f = mk.sb("ones1f_s", [1, 128], F32)
        ones1b = mk.sb("ones1b_s", [1, 128], BF16)
        csts = mk.sb("consts_s", [128, 8], F32)
        (C1, CNPI, CHPI, CEPS, CPI, CZERO, CNEG1, C64PI) = range(8)

        def cc(j):
            return csts.ap()[:, j:j + 1]

        cmask = mk.sb("cmask_s", [5, cfg["cps"]], F32)
        G_all = mk.sb("G_all", [128, NTOK], F32)
        d_all = mk.sb("d_all", [128, NTOK], F32)
        s_pos = mk.sb("s_pos", [128, NTOK], F32)
        s_kv = mk.sb("s_kv", [128, NTOK], F32)
        bw_tmp = mk.sb("bw_tmp", [128, NTOK], F32)
        gcol = mk.sb("gcol", [128, NT], F32)
        zi32 = mk.sb("zi32", [128, D], mybir.dt.int32)
        car_all = mk.sb("car_all", [1, 4 * D], F32)

        def car(i, sl=slice(0, D)):
            return car_all.ap()[0:1, i * D + sl.start:i * D + sl.stop]

        krows = mk.sb("krows", [1, 2 * D], F32)

        def kprev(i, sl=slice(0, D)):
            return krows.ap()[0:1, i * D + sl.start:i * D + sl.stop]

        xnprev = mk.sb("xnprev", [128, KC], BF16)
        stats = mk.sb("stats", [128, 12 * NT], F32)
        mv = mk.sb("mv", [128, 2 * NT], F32)
        colsc = mk.sb("colsc", [128, 4 * NT], F32)
        class _View:
            def __init__(self, buf, ap):
                self.buf = buf
                self._ap = ap
            def ap(self):
                return self._ap
        xprev_s = mk.sb("xprev_s", [1, D], F32)
        xprevf_s = mk.sb("xprevf_s", [128, KC], F32)
        bflag_s = mk.sb("bflag_s", [1, 1], F32)
        pmv = mk.sb("pmv", [1, 12], F32)
        pmv2 = mk.sb("pmv2", [1, 2], F32)
        prow1 = xprev_s

        def brow(i, sl=slice(0, D)):
            return brows.ap()[0:1, i * D + sl.start:i * D + sl.stop]

        bg1_c = mk.sb("bg1_c", [128, KC], F32)
        bb1_c = mk.sb("bb1_c", [128, 4], F32)
        bg2_c = mk.sb("bg2_c", [128, 1], F32)
        db2_c = mk.sb("db2_c", [128, 1], F32)
        lnb_c = mk.sb("lnb_c", [128, KC], BF16)
        lng_c = mk.sb("lng_c", [128, KC], F32)
        srow = mk.sb("srow", [1, 4], F32)
        def comb(i, sl=slice(0, D)):
            return combs.ap()[0:1, i * D + sl.start:i * D + sl.stop]

        pb = [mk.ps(f"pb{i}", [128, 512], F32) for i in range(4)]
        pf = [mk.ps(f"pf{i}", [128, Tg], F32) for i in range(2)]
        ptr = mk.ps("ptr", [128, 128], BF16)
        pcol = mk.ps("pcol", [128, 8], F32)

        FB = [mk.sb(f"fb{i}", [128, NT * D], F32) for i in range(6)]
        cb = _View(FB[3], FB[3].ap()[0:5, 0:D])
        carry5 = _View(FB[2], FB[2].ap()[0:5, 0:D])
        xn_tm = mk.sb("xn_tm", [128, NT * D], BF16)
        xn_fm = mk.sb("xn_fm", [128, KC * Tg], BF16)
        xn_sh = mk.sb("xn_sh", [128, KC * Tg], BF16)
        gh_fm = mk.sb("gh_fm", [128, KC * Tg], BF16)
        bh_fm = gh_fm
        ln_bf = xn_tm
        ln_fm = xn_sh
        q_fm = gh_fm
        ret_fm = xn_tm
        lp_f = mk.sb("lp_f", [128, NT * 256], F32)
        lp_bf = mk.sb("lp_bf", [128, NT * 256], BF16)
        lpT = mk.sb("lpT", [128, 2 * Tg], BF16)
        gate_f = mk.sb("gate_f", [128, NT * 255], F32)
        g2_f = mk.sb("g2_f", [128, NT * 510], F32)

        for (b, src) in [(tri, "tri"), (ident, "ident_hbm"),
                         (ones1f, "ones1f_hbm"), (ones1b, "ones1b_hbm"),
                         (csts, "consts_hbm"), (cmask, "cmask_hbm"),
                         (xprev_s, "xprev"), (xprevf_s, "xprev_fm"),
                         (bflag_s, "bflag")]:
            mk.dma("Y", b.ap(), di[src].ap(), writes=[b])

        mk.dma("G", lnb_c.ap(), di["lnb"].ap().rearrange("(c p) -> p c", p=128),
               writes=[lnb_c], slow=True)
        for b, src in [(bg1_c, "bg1"), (bb1_c, "bb1"), (lng_c, "lng")]:
            mk.dma("Y", b.ap(), di[src].ap().rearrange("(c p) -> p c", p=128),
                   writes=[b], slow=True)
        mk.dma("Y", srow.ap()[0:1, 0:1], di["bg2"].ap().unsqueeze(0),
               writes=[srow])
        mk.dma("Y", srow.ap()[0:1, 1:3], di["bb2"].ap().unsqueeze(0),
               reads=[srow], writes=[srow])
        mk.emit("V", lambda v: v.tensor_tensor(
            out=srow.ap()[0:1, 3:4], in0=srow.ap()[0:1, 2:3],
            in1=srow.ap()[0:1, 1:2], op=ALU.subtract),
            reads=[srow], writes=[srow])
        mk.emit("T", lambda t: (
            t.matmul(out=pcol.ap()[:, 0:1], lhsT=ones1f.ap(),
                     rhs=srow.ap()[0:1, 0:1], start=True, stop=True),
            t.matmul(out=pcol.ap()[:, 1:2], lhsT=ones1f.ap(),
                     rhs=srow.ap()[0:1, 3:4], start=True, stop=True))[-1],
            reads=[ones1f, srow], writes=[pcol])
        mk.emit("S", lambda s: (
            s.copy(out=bg2_c.ap(), in_=pcol.ap()[:, 0:1]),
            s.copy(out=db2_c.ap(), in_=pcol.ap()[:, 1:2]))[-1],
            reads=[pcol], writes=[bg2_c, db2_c])

        def mkcols_s(s):
            last = None
            for t in range(NT):
                last = s.activation(
                    out=colsc.ap()[:, 4 * t:4 * t + 1],
                    in_=mv.ap()[:, 2 * t + 1:2 * t + 2], func=AF.Sqrt,
                    bias=cc(CEPS), scale=cc(C1))
            return last

        saved_stack = mk.stack
        if mode in ("fused", "front"):
            wstack = ExitStack()
            mk.stack = wstack
            Wpv = mk.sb("Wpv", [128, KC * D], BF16)
            Wk = mk.sb("Wk", [128, KC * D], BF16)
            Wkv = mk.sb("Wkv", [128, KC * D], BF16)
            Wg1 = mk.sb("Wg1", [128, 16 * D], BF16)
            Wb1 = mk.sb("Wb1", [128, KC * 512], BF16)
            Wg2c = mk.sb("Wg2c", [128, KC], BF16)
            dWb2 = mk.sb("dWb2", [128, 4], BF16)
            wb2_st = mk.sb("wb2_st", [128, 8], F32)
            brows = mk.sb("brows", [1, 3 * D], BF16)
            mk.stack = saved_stack

            for i, src in [(0, "bpv"), (1, "bk"), (2, "bkv")]:
                mk.dma("G", brow(i), di[src].ap().unsqueeze(0),
                       reads=[brows], writes=[brows])

            for wb, src, nw in [(Wpv, "wpv", D), (Wk, "wk", D), (Wkv, "wkv", D),
                                (Wg1, "wg1", D), (Wb1, "wb1", 512)]:
                mk.dma("G", wb.ap().rearrange("p (kc o) -> p kc o", o=nw),
                       di[src].ap().rearrange("(kc kp) o -> kp kc o", kp=128),
                       writes=[wb])
            mk.dma("G", Wg2c.ap(),
                   di["wg2"].ap().rearrange("(c p) o -> p c o", p=128).squeeze(2),
                   writes=[Wg2c], slow=True)
            mk.dma("Y", wb2_st.ap().rearrange("p (c o) -> p c o", o=2),
                   di["wb2"].ap().rearrange("(c p) o -> p c o", p=128),
                   writes=[wb2_st], slow=True)
            mk.emit("V", lambda v: v.tensor_tensor(
                out=dWb2.ap(),
                in0=wb2_st.ap().rearrange("p (c o) -> p c o", o=2)[:, :, 1:2].squeeze(2),
                in1=wb2_st.ap().rearrange("p (c o) -> p c o", o=2)[:, :, 0:1].squeeze(2),
                op=ALU.subtract), reads=[wb2_st], writes=[dWb2])

            mk.emit("V", lambda v: (
                v.bn_stats(out=pmv.ap()[0:1, 0:6], in_=xprev_s.ap()[0:1, 0:512]),
                v.bn_stats(out=pmv.ap()[0:1, 6:12], in_=xprev_s.ap()[0:1, 512:1024]),
                v.drain(),
                v.bn_aggr(out=pmv2.ap(), in_=pmv.ap()))[-1],
                reads=[xprev_s], writes=[pmv, pmv2])
            mk.emit("S", lambda s: s.activation(
                out=pmv.ap()[0:1, 0:1], in_=pmv2.ap()[0:1, 1:2], func=AF.Sqrt,
                bias=cc(CEPS)[0:1, :], scale=cc(C1)[0:1, :]),
                reads=[pmv2, csts], writes=[pmv])
            mk.emit("V", lambda v: v.reciprocal(
                out=pmv.ap()[0:1, 0:1], in_=pmv.ap()[0:1, 0:1]),
                reads=[pmv], writes=[pmv])
            mk.emit("V", lambda v: v.tensor_scalar(
                out=pmv.ap()[0:1, 1:2], in0=pmv2.ap()[0:1, 0:1],
                scalar1=pmv.ap()[0:1, 0:1], scalar2=-1.0,
                op0=ALU.mult, op1=ALU.mult),
                reads=[pmv2, pmv], writes=[pmv])
            mk.emit("T", lambda t: t.matmul(
                out=pcol.ap()[:, 2:4], lhsT=ones1f.ap(), rhs=pmv.ap()[0:1, 0:2],
                start=True, stop=True), reads=[ones1f, pmv], writes=[pcol])
            mk.emit("S", lambda s: s.copy(out=colsc.ap()[:, 0:2],
                                          in_=pcol.ap()[:, 2:4]),
                    reads=[pcol], writes=[colsc])
            mk.emit("S", lambda s: s.activation(
                out=xnprev.ap(), in_=xprevf_s.ap(), func=AF.Identity,
                scale=colsc.ap()[:, 0:1], bias=colsc.ap()[:, 1:2]),
                reads=[xprevf_s, colsc], writes=[xnprev])

            def pre_key(t):
                last = None
                for c in range(KC):
                    for h in range(2):
                        last = t.matmul(
                            out=pb[h].ap()[0:1, :],
                            lhsT=xnprev.ap()[:, c:c + 1],
                            rhs=Wk.ap()[:, c * D + h * 512: c * D + (h + 1) * 512],
                            start=(c == 0), stop=False)
                for h in range(2):
                    last = t.matmul(out=pb[h].ap()[0:1, :],
                                    lhsT=ones1b.ap()[0:1, 0:1],
                                    rhs=brow(1, slice(h * 512, (h + 1) * 512)),
                                    start=False, stop=True)
                return last
            mk.emit("T", pre_key, reads=[xnprev, Wk, ones1b, brows],
                    writes=[pb[0], pb[1]])
            mk.emit("S", lambda s: (
                s.activation(out=prow1.ap()[0:1, 0:512], in_=pb[0].ap()[0:1, :],
                             func=AF.Tanh),
                s.activation(out=prow1.ap()[0:1, 512:1024], in_=pb[1].ap()[0:1, :],
                             func=AF.Tanh))[-1],
                reads=[pb[0], pb[1]], writes=[prow1])
            mk.emit("S", lambda s: s.activation(
                out=kprev(1), in_=prow1.ap(), func=AF.Sin,
                scale=cc(CPI)[0:1, :], bias=cc(CZERO)[0:1, :]),
                reads=[prow1, csts], writes=[krows])
            mk.emit("V", lambda v: v.tensor_scalar(
                out=kprev(1), in0=kprev(1), scalar1=bflag_s.ap()[0:1, 0:1],
                scalar2=None, op0=ALU.mult), reads=[krows, bflag_s], writes=[krows])
            mk.emit("S", lambda s: s.activation(
                out=prow1.ap(), in_=prow1.ap(), func=AF.Abs),
                reads=[prow1], writes=[prow1])
            mk.emit("S", lambda s: s.activation(
                out=kprev(0), in_=prow1.ap(), func=AF.Sin,
                scale=cc(CNPI)[0:1, :], bias=cc(CHPI)[0:1, :]),
                reads=[prow1, csts, krows], writes=[krows])
            mk.emit("V", lambda v: v.tensor_scalar(
                out=kprev(0), in0=kprev(0), scalar1=bflag_s.ap()[0:1, 0:1],
                scalar2=None, op0=ALU.mult), reads=[krows, bflag_s], writes=[krows])

            for g in range(NG):
                xg, par, ph, sp_b, cp_b, b5 = FB
                mk.dma("Y", sb3(xg), di["x"].ap()[g * Tg:(g + 1) * Tg, :]
                       .rearrange("(t p) f -> p t f", p=128),
                       reads=[xg], writes=[xg])
                mk.dma("Y", sb3(ph), di["ph"].ap()[g * Tg:(g + 1) * Tg, :]
                       .rearrange("(t p) f -> p t f", p=128),
                       reads=[ph], writes=[ph])

                def bnst(v):
                    for t in range(NT):
                        for hh in range(2):
                            v.bn_stats(out=stats.ap()[:, 12 * t + 6 * hh:
                                                      12 * t + 6 * hh + 6],
                                       in_=xg.ap()[:, t * D + hh * 512:
                                                   t * D + (hh + 1) * 512])
                    v.drain()
                    last = None
                    for t in range(NT):
                        last = v.bn_aggr(out=mv.ap()[:, 2 * t:2 * t + 2],
                                         in_=stats.ap()[:, 12 * t:12 * t + 12])
                    return last
                mk.emit("V", bnst, reads=[xg], writes=[stats, mv])

                mk.emit("S", mkcols_s, reads=[mv, csts], writes=[colsc])

                def mkcols_v(v):
                    for t in range(NT):
                        v.reciprocal(out=colsc.ap()[:, 4 * t:4 * t + 1],
                                     in_=colsc.ap()[:, 4 * t:4 * t + 1])
                    v.drain()
                    for t in range(NT):
                        v.tensor_scalar(
                            out=colsc.ap()[:, 4 * t + 1:4 * t + 2],
                            in0=mv.ap()[:, 2 * t:2 * t + 1],
                            scalar1=colsc.ap()[:, 4 * t:4 * t + 1], scalar2=-1.0,
                            op0=ALU.mult, op1=ALU.mult)
                    v.drain()
                    last = None
                    for t in range(NT):
                        last = v.tensor_scalar(
                            out=colsc.ap()[:, 4 * t + 2:4 * t + 3],
                            in0=colsc.ap()[:, 4 * t:4 * t + 1], scalar1=1.0,
                            scalar2=None, op0=ALU.add)
                    return last
                mk.emit("V", mkcols_v, reads=[mv, colsc], writes=[colsc])

                def xnact(s):
                    last = None
                    for t in range(NT):
                        s.activation(out=xn_tm.ap()[:, t * D:(t + 1) * D],
                                     in_=xg.ap()[:, t * D:(t + 1) * D],
                                     func=AF.Identity, scale=colsc.ap()[:, 4 * t:4 * t + 1],
                                     bias=colsc.ap()[:, 4 * t + 1:4 * t + 2])
                        last = s.activation(
                            out=par.ap()[:, t * D:(t + 1) * D],
                            in_=xg.ap()[:, t * D:(t + 1) * D], func=AF.Identity,
                            scale=colsc.ap()[:, 4 * t + 2:4 * t + 3],
                            bias=colsc.ap()[:, 4 * t + 1:4 * t + 2])
                    return last
                mk.emit("S", xnact, reads=[xg, colsc], writes=[xn_tm, par])
                mk.dma("Y", spl(spd["par_sp"], g), sb3(par), reads=[par])

                for t in range(NT):
                    for c in range(KC):
                        mk.emit("T", lambda tt, t=t, c=c: tt.transpose(
                            out=ptr.ap(), in_=xn_tm.ap()[:, t * D + c * 128:
                                                         t * D + (c + 1) * 128],
                            identity=ident.ap()),
                            reads=[xn_tm, ident], writes=[ptr])
                        mk.emit("S", lambda s, t=t, c=c: s.copy(
                            out=xn_fm.ap()[:, c * Tg + t * 128:c * Tg + t * 128 + 128],
                            in_=ptr.ap()), reads=[ptr], writes=[xn_fm])
                mk.dma("Y", xnfm_sp.ap()[g * 128:(g + 1) * 128, :], xn_fm.ap(),
                       reads=[xn_fm])

                def shx(v):
                    v.tensor_copy(
                        out=xn_sh.ap().rearrange("p (c t) -> p c t", c=KC)[:, :, 1:Tg],
                        in_=xn_fm.ap().rearrange("p (c t) -> p c t", c=KC)[:, :, 0:Tg - 1])
                    last = v.tensor_copy(
                        out=xn_sh.ap().rearrange("p (c t) -> p c t", c=KC)[:, :, 0:1].squeeze(2),
                        in_=xnprev.ap())
                    return last
                mk.emit("V", shx, reads=[xn_fm, xnprev], writes=[xn_sh])
                mk.emit("V", lambda v: v.tensor_copy(
                    out=xnprev.ap(),
                    in_=xn_fm.ap().rearrange("p (c t) -> p c t", c=KC)[:, :, Tg - 1:Tg].squeeze(2)),
                    reads=[xn_fm], writes=[xnprev])

                def tm_mm(Wb, bi, t):
                    def body(tt, Wb=Wb, bi=bi, t=t):
                        last = None
                        for c in range(KC):
                            for h in range(2):
                                last = tt.matmul(
                                    out=pb[h].ap(),
                                    lhsT=xn_fm.ap()[:, c * Tg + t * 128:
                                                    c * Tg + t * 128 + 128],
                                    rhs=Wb.ap()[:, c * D + h * 512:c * D + (h + 1) * 512],
                                    start=(c == 0), stop=False)
                        for h in range(2):
                            last = tt.matmul(
                                out=pb[h].ap(), lhsT=ones1b.ap(),
                                rhs=brow(bi, slice(h * 512, (h + 1) * 512)),
                                start=False, stop=True)
                        return last
                    return body

                def wrapv(v):
                    last = None
                    for hh in range(2):
                        sl = slice(hh * D, (hh + 1) * D)
                        v.tensor_scalar(out=zi32.ap(), in0=ph.ap()[:, sl],
                                        scalar1=1.0 / (2 * PI), scalar2=32.0,
                                        op0=ALU.mult, op1=ALU.add)
                        v.drain()
                        v.scalar_tensor_tensor(out=ph.ap()[:, sl], in0=zi32.ap(),
                                               scalar=-2 * PI, in1=ph.ap()[:, sl],
                                               op0=ALU.mult, op1=ALU.add)
                        v.drain()
                        v.tensor_scalar(out=sp_b.ap()[:, sl], in0=ph.ap()[:, sl],
                                        scalar1=-63 * PI, scalar2=-2 * PI,
                                        op0=ALU.is_gt, op1=ALU.mult)
                        v.drain()
                        v.tensor_tensor(out=ph.ap()[:, sl],
                                        in0=ph.ap()[:, sl],
                                        in1=sp_b.ap()[:, sl], op=ALU.add)
                        v.drain()
                        last = v.tensor_scalar(
                            out=ph.ap()[:, sl], in0=ph.ap()[:, sl],
                            scalar1=-65 * PI + 5e-5, scalar2=-63 * PI - 5e-5,
                            op0=ALU.max, op1=ALU.min)
                        if hh == 0:
                            v.drain()
                    return last
                mk.emit("V", wrapv, reads=[ph, sp_b], writes=[ph, sp_b, zi32])
                mk.emit("S", lambda s: s.activation(
                    out=sp_b.ap(), in_=ph.ap(), func=AF.Sin,
                    bias=cc(C64PI), scale=cc(C1)),
                    reads=[ph, csts], writes=[sp_b])
                mk.emit("S", lambda s: s.activation(
                    out=ph.ap(), in_=ph.ap(), func=AF.Abs, bias=cc(C64PI),
                    scale=cc(C1)), reads=[ph, csts], writes=[ph])
                mk.emit("S", lambda s: s.activation(
                    out=cp_b.ap(), in_=ph.ap(), func=AF.Sin,
                    scale=cc(CNEG1), bias=cc(CHPI)),
                    reads=[ph, csts], writes=[cp_b])
                mk.dma("Y", spl(spd["cs_sp"], g), sb3(cp_b), reads=[cp_b])
                mk.dma("Y", spl(spd["sn_sp"], g), sb3(sp_b), reads=[sp_b])

                for t in range(NT):
                    mk.emit("T", tm_mm(Wpv, 0, t),
                            reads=[xn_fm, Wpv, ones1b, brows], writes=[pb[0], pb[1]])
                    for h in range(2):
                        sl = slice(t * D + h * 512, t * D + (h + 1) * 512)
                        mk.emit("V", lambda v, sl=sl, h=h: v.tensor_tensor(
                            out=sp_b.ap()[:, sl], in0=sp_b.ap()[:, sl],
                            in1=pb[h].ap(), op=ALU.mult),
                            reads=[sp_b, pb[h]], writes=[sp_b])
                        mk.emit("V", lambda v, sl=sl, h=h: v.tensor_tensor(
                            out=cp_b.ap()[:, sl], in0=cp_b.ap()[:, sl],
                            in1=pb[h].ap(), op=ALU.mult),
                            reads=[cp_b, pb[h]], writes=[cp_b])

                def scan(buf, ci, g=g):
                    for t in range(NT):
                        first = (t == 0 and g == 0)
                        def smm(tt, buf=buf, ci=ci, t=t, first=first):
                            last = None
                            for h in range(2):
                                last = tt.matmul(
                                    out=pb[2 + h].ap(), lhsT=tri.ap(),
                                    rhs=buf.ap()[:, t * D + h * 512:
                                                 t * D + (h + 1) * 512],
                                    start=True, stop=first)
                                if not first:
                                    crow = car(ci, slice(h * 512, (h + 1) * 512))
                                    last = tt.matmul(out=pb[2 + h].ap(),
                                                     lhsT=ones1f.ap(), rhs=crow,
                                                     start=False, stop=True)
                            return last
                        mk.emit("T", smm, reads=[buf, tri, ones1f, car_all],
                                writes=[pb[2], pb[3]])
                        mk.emit("S", lambda s, buf=buf, t=t: (
                            s.copy(out=buf.ap()[:, t * D:t * D + 512], in_=pb[2].ap()),
                            s.copy(out=buf.ap()[:, t * D + 512:(t + 1) * D],
                                   in_=pb[3].ap()))[-1],
                            reads=[pb[2], pb[3]], writes=[buf])
                        mk.dma("G", car(ci), buf.ap()[127:128, t * D:(t + 1) * D],
                               reads=[buf, car_all], writes=[car_all])

                scan(cp_b, 0)
                scan(sp_b, 1)
                mk.dma("Y", spl(spd["a_sp"], g), sb3(cp_b), reads=[cp_b])
                mk.dma("Y", spl(spd["b_sp"], g), sb3(sp_b), reads=[sp_b])

                for t in range(NT):
                    mk.emit("T", tm_mm(Wk, 1, t),
                            reads=[xn_fm, Wk, ones1b, brows], writes=[pb[0], pb[1]])
                    mk.emit("S", lambda s, t=t: (
                        s.activation(out=xg.ap()[:, t * D:t * D + 512],
                                     in_=pb[0].ap(), func=AF.Tanh),
                        s.activation(out=xg.ap()[:, t * D + 512:(t + 1) * D],
                                     in_=pb[1].ap(), func=AF.Tanh))[-1],
                        reads=[pb[0], pb[1]], writes=[xg])
                mk.emit("S", lambda s: s.activation(
                    out=ph.ap(), in_=xg.ap(), func=AF.Sin, scale=cc(CPI),
                    bias=cc(CZERO)), reads=[xg, csts], writes=[ph])
                mk.emit("S", lambda s: s.activation(
                    out=xg.ap(), in_=xg.ap(), func=AF.Abs),
                    reads=[xg], writes=[xg])
                mk.emit("S", lambda s: s.activation(
                    out=par.ap(), in_=xg.ap(), func=AF.Sin,
                    scale=cc(CNPI), bias=cc(CHPI)),
                    reads=[xg, csts], writes=[par])
                mk.dma("Y", spl(spd["kc_sp"], g), sb3(par), reads=[par])
                mk.dma("Y", spl(spd["ks_sp"], g), sb3(ph), reads=[ph])

                for (srcb, dstb, ki) in ((par, cp_b, 0), (ph, b5, 1)):
                    for t in range(NT):
                        if t == 0:
                            mk.dma("G", dstb.ap()[0:1, 0:D], kprev(ki),
                                   reads=[krows, dstb], writes=[dstb])
                        else:
                            mk.dma("G", dstb.ap()[0:1, t * D:(t + 1) * D],
                                   srcb.ap()[127:128, (t - 1) * D:t * D],
                                   reads=[srcb, dstb], writes=[dstb])
                        mk.dma("G", dstb.ap()[1:128, t * D:(t + 1) * D],
                               srcb.ap()[0:127, t * D:(t + 1) * D],
                               reads=[srcb, dstb], writes=[dstb])
                    mk.dma("G", kprev(ki),
                           srcb.ap()[127:128, (NT - 1) * D:NT * D],
                           reads=[srcb, krows], writes=[krows])

                for o in range(KC):
                    bank = pf[o % 2]
                    def ghmm_o(tt, o=o, bank=bank):
                        last = None
                        for k in range(16):
                            src = xn_fm if k < 8 else xn_sh
                            kc_ = k % 8
                            last = tt.matmul(
                                out=bank.ap(),
                                lhsT=Wg1.ap()[:, k * D + o * 128:k * D + (o + 1) * 128],
                                rhs=src.ap()[:, kc_ * Tg:(kc_ + 1) * Tg],
                                start=(k == 0), stop=(k == 15))
                        return last
                    mk.emit("T", ghmm_o, reads=[Wg1, xn_fm, xn_sh], writes=[bank])
                    mk.emit("S", lambda s, o=o, bank=bank: s.activation(
                        out=gh_fm.ap()[:, o * Tg:(o + 1) * Tg], in_=bank.ap(),
                        func=AF.Gelu, bias=bg1_c.ap()[:, o:o + 1],
                        scale=cc(C1)),
                        reads=[bank, bg1_c, csts], writes=[gh_fm])
                for t in range(NT):
                    def vgmm(tt, t=t):
                        last = None
                        for e in range(KC):
                            last = tt.matmul(
                                out=pcol.ap()[:, 4 + t:5 + t],
                                lhsT=gh_fm.ap()[:, e * Tg + t * 128:e * Tg + t * 128 + 128],
                                rhs=Wg2c.ap()[:, e:e + 1],
                                start=(e == 0), stop=(e == KC - 1))
                        return last
                    mk.emit("T", vgmm, reads=[gh_fm, Wg2c], writes=[pcol])
                    mk.emit("S", lambda s, t=t: s.activation(
                        out=gcol.ap()[:, t:t + 1], in_=pcol.ap()[:, 4 + t:5 + t],
                        func=AF.Sigmoid, bias=bg2_c.ap(), scale=cc(C1)),
                        reads=[pcol, bg2_c, csts], writes=[gcol])

                for t in range(NT):
                    mk.emit("T", tm_mm(Wkv, 2, t),
                            reads=[xn_fm, Wkv, ones1b, brows], writes=[pb[0], pb[1]])
                    mk.emit("S", lambda s, t=t: (
                        s.activation(out=sp_b.ap()[:, t * D:t * D + 512],
                                     in_=pb[0].ap(), func=AF.Identity,
                                     scale=gcol.ap()[:, t:t + 1]),
                        s.activation(out=sp_b.ap()[:, t * D + 512:(t + 1) * D],
                                     in_=pb[1].ap(), func=AF.Identity,
                                     scale=gcol.ap()[:, t:t + 1]))[-1],
                        reads=[pb[0], pb[1], gcol], writes=[sp_b])
                mk.emit("V", lambda v: v.tensor_tensor(
                    out=cp_b.ap(), in0=cp_b.ap(), in1=sp_b.ap(), op=ALU.mult),
                    reads=[cp_b, sp_b], writes=[cp_b])
                mk.emit("V", lambda v: v.tensor_tensor(
                    out=b5.ap(), in0=b5.ap(), in1=sp_b.ap(), op=ALU.mult),
                    reads=[b5, sp_b], writes=[b5])
                scan(cp_b, 2)
                scan(b5, 3)
                mk.dma("Y", spl(spd["u_sp"], g), sb3(cp_b), reads=[cp_b])
                mk.dma("Y", spl(spd["v_sp"], g), sb3(b5), reads=[b5])

                for t in range(NT):
                    col = g * NT + t
                    def gsc(tt, t=t, col=col):
                        last = tt.matmul(out=pcol.ap()[:, 6:7], lhsT=tri.ap(),
                                         rhs=gcol.ap()[:, t:t + 1], start=True,
                                         stop=(col == 0))
                        if col > 0:
                            last = tt.matmul(out=pcol.ap()[:, 6:7], lhsT=ones1f.ap(),
                                             rhs=srow.ap()[0:1, 1:2],
                                             start=False, stop=True)
                        return last
                    mk.emit("T", gsc, reads=[tri, gcol, ones1f, srow],
                            writes=[pcol])
                    mk.emit("S", lambda s, col=col: s.copy(
                        out=G_all.ap()[:, col:col + 1], in_=pcol.ap()[:, 6:7]),
                        reads=[pcol], writes=[G_all])
                    mk.dma("G", srow.ap()[0:1, 1:2],
                           G_all.ap()[127:128, col:col + 1],
                           reads=[G_all, srow], writes=[srow])

                for o in range(4):
                    bank = pf[o % 2]
                    def bhmm(tt, o=o, bank=bank):
                        last = None
                        for k in range(KC):
                            last = tt.matmul(
                                out=bank.ap(),
                                lhsT=Wb1.ap()[:, k * 512 + o * 128:k * 512 + (o + 1) * 128],
                                rhs=xn_fm.ap()[:, k * Tg:(k + 1) * Tg],
                                start=(k == 0), stop=(k == KC - 1))
                        return last
                    mk.emit("T", bhmm, reads=[Wb1, xn_fm], writes=[bank])
                    mk.emit("S", lambda s, o=o, bank=bank: s.activation(
                        out=bh_fm.ap()[:, o * Tg:(o + 1) * Tg], in_=bank.ap(),
                        func=AF.Gelu, bias=bb1_c.ap()[:, o:o + 1],
                        scale=cc(C1)),
                        reads=[bank, bb1_c, csts], writes=[bh_fm])
                for t in range(NT):
                    col = g * NT + t
                    def dmm(tt, t=t):
                        last = None
                        for e in range(4):
                            last = tt.matmul(
                                out=pcol.ap()[:, 7:8],
                                lhsT=bh_fm.ap()[:, e * Tg + t * 128:e * Tg + t * 128 + 128],
                                rhs=dWb2.ap()[:, e:e + 1],
                                start=(e == 0), stop=(e == 3))
                        return last
                    mk.emit("T", dmm, reads=[bh_fm, dWb2], writes=[pcol])
                    mk.emit("S", lambda s, col=col: s.activation(
                        out=d_all.ap()[:, col:col + 1], in_=pcol.ap()[:, 7:8],
                        func=AF.Identity, bias=db2_c.ap(), scale=cc(C1)),
                        reads=[pcol, db2_c, csts], writes=[d_all])

        if mode in ("fused", "front"):
            mk.dma("G", cc_in.ap()[0:4, :].rearrange("(o r) d -> o r d", o=1),
                   car_all.ap().rearrange("o (i d) -> o i d", d=D),
                   reads=[car_all], writes=[cc_in])
            mk.dma("G", cc_in.ap()[4:5, :].rearrange("(o r) d -> o r d", o=1),
                   car_all.ap()[0:1, 0:D].unsqueeze(1),
                   reads=[car_all, cc_in], writes=[cc_in])
            ev_g = mk.dma("G", cc_in.ap()[4:5, 0:1],
                          G_all.ap()[127:128, NTOK - 1:NTOK],
                          reads=[G_all, cc_in], writes=[cc_in])

        if mode == "front":
            mk.dma("Y", ga_out.ap(), G_all.ap(), reads=[G_all])
            mk.dma("Y", da_out.ap(), d_all.ap(), reads=[d_all])
        cc_buf = Buf("ccb", None)
        if mode == "fused":
            rg = [[c + s * cfg["cps"] for c in range(cfg["cps"])]
                  for s in range(cfg["B"])]
            mk.emit("G", lambda gg: gg.collective_compute(
                "AllGather", ALU.bypass, replica_groups=rg,
                ins=[cc_in.ap()], outs=[cc_out.ap()]),
                deps=[ev_g], reads=[cc_in], writes=[cc_buf],
                sig="CC", n_inc=16)
        if mode == "back":
            mk.dma("Y", G_all.ap(), ga_in.ap(), reads=[G_all], writes=[G_all])
            mk.dma("Y", d_all.ap(), da_in.ap(), reads=[d_all], writes=[d_all])
        if mode in ("fused", "back"):
            mk.emit("V", lambda v: v.memset(carry5.ap(), 0.0), writes=[carry5.buf])
            for r in range(cfg["cps"]):
                mk.dma("G", cb.ap(), cc_out.ap()[r * 5:(r + 1) * 5, :],
                       reads=[cc_buf, cb.buf], writes=[cb.buf])
                mk.emit("V", lambda v, r=r: v.scalar_tensor_tensor(
                    out=carry5.ap(), in0=cb.ap(), scalar=cmask.ap()[:, r:r + 1],
                    in1=carry5.ap(), op0=ALU.mult, op1=ALU.add),
                    reads=[cb.buf, cmask, carry5.buf], writes=[carry5.buf])
            for i in range(4):
                mk.dma("G", car(i), carry5.ap()[i:i + 1, :],
                       reads=[carry5.buf, car_all], writes=[car_all])
            mk.dma("G", srow.ap()[0:1, 2:3], carry5.ap()[4:5, 0:1],
                   reads=[carry5.buf, srow], writes=[srow])
            cstg = FB[4]
            for i in range(4):
                def cbm2(tt, i=i):
                    tt.matmul(out=pb[0].ap(), lhsT=ones1f.ap(),
                              rhs=car(i, slice(0, 512)), start=True, stop=True)
                    return tt.matmul(out=pb[1].ap(), lhsT=ones1f.ap(),
                                     rhs=car(i, slice(512, 1024)),
                                     start=True, stop=True)
                mk.emit("T", cbm2, reads=[ones1f, car_all], writes=[pb[0], pb[1]])
                mk.emit("S", lambda s, i=i: (
                    s.copy(out=cstg.ap()[:, 0:512], in_=pb[0].ap()),
                    s.copy(out=cstg.ap()[:, 512:1024], in_=pb[1].ap()))[-1],
                    reads=[pb[0], pb[1]], writes=[cstg])
                mk.dma("G", cbc_dram.ap()[i * 128:(i + 1) * 128, :],
                       cstg.ap()[:, 0:D], reads=[cstg], writes=[cbc_buf[i]])
            mk.emit("T", lambda t: t.matmul(
                out=pcol.ap()[:, 0:1], lhsT=ones1f.ap(), rhs=srow.ap()[0:1, 2:3],
                start=True, stop=True), reads=[ones1f, srow], writes=[pcol])
            mk.emit("S", lambda s: s.copy(out=colsc.ap()[:, 0:1],
                                          in_=pcol.ap()[:, 0:1]),
                    reads=[pcol], writes=[colsc])
            mk.emit("V", lambda v: v.tensor_scalar(
                out=G_all.ap(), in0=G_all.ap(), scalar1=colsc.ap()[:, 0:1],
                scalar2=1.0, op0=ALU.add, op1=ALU.max),
                reads=[G_all, colsc], writes=[G_all])
            mk.emit("S", lambda s: s.activation(
                out=bw_tmp.ap(), in_=G_all.ap(), func=AF.Sqrt,
                bias=cc(CZERO), scale=cc(C1)),
                reads=[G_all, csts], writes=[bw_tmp])
            mk.emit("V", lambda v: v.reciprocal(out=bw_tmp.ap(), in_=bw_tmp.ap()),
                    reads=[bw_tmp], writes=[bw_tmp])
            mk.emit("S", lambda s: s.activation(
                out=s_kv.ap(), in_=d_all.ap(), func=AF.Sigmoid),
                reads=[d_all], writes=[s_kv])
            mk.emit("S", lambda s: s.activation(
                out=s_pos.ap(), in_=d_all.ap(), func=AF.Sigmoid,
                scale=cc(CNEG1), bias=cc(CZERO)),
                reads=[d_all, csts], writes=[s_pos])
            mk.emit("V", lambda v: (v.tensor_tensor(
                out=s_kv.ap(), in0=s_kv.ap(), in1=bw_tmp.ap(), op=ALU.mult),
                v.drain(),
                v.tensor_scalar_mul(out=s_kv.ap(), in0=s_kv.ap(), scalar1=1.0 / 32),
                v.drain(),
                v.tensor_scalar_mul(out=s_pos.ap(), in0=s_pos.ap(), scalar1=1.0 / 32),
            )[-1], reads=[s_kv, bw_tmp, s_pos], writes=[s_kv, s_pos])

            if mode == "fused":
                wstack.close()
            wstack2 = ExitStack()
            mk.stack = wstack2
            Wo = mk.sb("Wo", [128, KC * D], BF16)
            WqT = mk.sb("WqT", [128, KC * D], BF16)
            WoutT = mk.sb("WoutT", [128, KC * D], BF16)
            routT = mk.sb("routT", [128, KC * 255], BF16)
            leaf = mk.sb("leaf", [128, 2 * D], BF16)
            wstage = mk.sb("wstage", [128, KC * D], BF16)
            ro_st = mk.sb("ro_st", [128, 2 * D], BF16)
            brow_p = mk.sb("brow_p", [1, D], F32)
            combs = mk.sb("combs", [1, 2 * D], BF16)
            mk.stack = saved_stack
            ev_front_T = Ev(("T", mk.ctr["T"]))

            mk.dma("G", Wo.ap().rearrange("p (kc o) -> p kc o", o=D),
                   di["wo"].ap().rearrange("(kc kp) o -> kp kc o", kp=128),
                   deps=[ev_front_T], writes=[Wo])
            mk.dma("G", leaf.ap().rearrange("p (kc o) -> p kc o", o=D),
                   di["lfv"].ap().rearrange("(kc kp) o -> kp kc o", kp=128),
                   deps=[ev_front_T], writes=[leaf])
            mk.dma("Y", cb.ap()[0:1, :], di["bo"].ap().unsqueeze(0),
                   reads=[cb.buf], writes=[cb.buf])
            mk.dma("Y", carry5.ap()[0:1, :], di["woutb"].ap().unsqueeze(0),
                   reads=[carry5.buf], writes=[carry5.buf])

            def browmm(tt):
                last = None
                for k in range(KC):
                    for h in range(2):
                        last = tt.matmul(
                            out=(pb[2] if h == 0 else pb[3]).ap()[0:1, :],
                            lhsT=lnb_c.ap()[:, k:k + 1],
                            rhs=Wo.ap()[:, k * D + h * 512:k * D + (h + 1) * 512],
                            start=(k == 0), stop=(k == KC - 1))
                return last
            mk.emit("T", browmm, reads=[lnb_c, Wo], writes=[pb[2], pb[3]])
            mk.emit("S", lambda s: (
                s.copy(out=brow_p.ap()[0:1, 0:512], in_=pb[2].ap()[0:1, :]),
                s.copy(out=brow_p.ap()[0:1, 512:1024], in_=pb[3].ap()[0:1, :]))[-1],
                reads=[pb[2], pb[3]], writes=[brow_p])
            mk.emit("V", lambda v: (
                v.tensor_tensor(out=brow_p.ap(), in0=brow_p.ap(),
                                in1=cb.ap()[0:1, :], op=ALU.add),
                v.drain(),
                v.tensor_tensor(out=brow_p.ap(), in0=brow_p.ap(),
                                in1=carry5.ap()[0:1, :], op=ALU.add),
                v.drain(),
                v.tensor_copy(out=comb(0), in_=brow_p.ap()),
                v.drain(),
                v.tensor_copy(out=prow1.ap(), in_=comb(0)),
                v.drain(),
                v.tensor_tensor(out=prow1.ap(), in0=brow_p.ap(), in1=prow1.ap(),
                                op=ALU.subtract),
                v.drain(),
                v.tensor_copy(out=comb(1), in_=prow1.ap()),
            )[-1], reads=[brow_p, cb.buf, carry5.buf],
                writes=[brow_p, combs, prow1])

            def wo_scale(v):
                last = None
                for k in range(KC):
                    last = v.tensor_scalar(
                        out=Wo.ap()[:, k * D:(k + 1) * D],
                        in0=Wo.ap()[:, k * D:(k + 1) * D],
                        scalar1=lng_c.ap()[:, k:k + 1], scalar2=None, op0=ALU.mult)
                return last
            mk.emit("V", wo_scale, reads=[Wo, lng_c], writes=[Wo])

            def trload(dst, src_name):
                mk.dma("G", wstage.ap().rearrange("p (ec d) -> p ec d", d=D),
                       di[src_name].ap().rearrange("(ec ep) d -> ep ec d", ep=128),
                       deps=[ev_front_T], reads=[wstage], writes=[wstage])
                for ec in range(KC):
                    for dc in range(KC):
                        mk.emit("T", lambda t, ec=ec, dc=dc: t.transpose(
                            out=ptr.ap(),
                            in_=wstage.ap()[:, ec * D + dc * 128:ec * D + (dc + 1) * 128],
                            identity=ident.ap()),
                            reads=[wstage, ident], writes=[ptr])
                        mk.emit("S", lambda s, ec=ec, dc=dc, dst=dst: s.copy(
                            out=dst.ap()[:, dc * D + ec * 128:dc * D + (ec + 1) * 128],
                            in_=ptr.ap()), reads=[ptr], writes=[dst])
            trload(WqT, "wq")
            trload(WoutT, "wout")
            mk.dma("G", ro_st.ap()[0:128, 0:D], di["rout"].ap()[0:128, :],
                   deps=[ev_front_T], reads=[ro_st], writes=[ro_st])
            mk.dma("G", ro_st.ap()[0:127, D:2 * D], di["rout"].ap()[128:255, :],
                   reads=[ro_st], writes=[ro_st])
            for nch in range(2):
                nrows = 128 if nch == 0 else 127
                for ec in range(KC):
                    mk.emit("T", lambda t, nch=nch, ec=ec, nrows=nrows: t.transpose(
                        out=ptr.ap()[:, 0:nrows],
                        in_=ro_st.ap()[0:nrows, nch * D + ec * 128:nch * D + (ec + 1) * 128],
                        identity=ident.ap()[0:nrows, 0:nrows]),
                        reads=[ro_st, ident], writes=[ptr])
                    mk.emit("S", lambda s, nch=nch, ec=ec, nrows=nrows: s.copy(
                        out=routT.ap()[:, ec * 255 + nch * 128:
                                       ec * 255 + nch * 128 + nrows],
                        in_=ptr.ap()[:, 0:nrows]), reads=[ptr], writes=[routT])

            for g in range(NG):
                pr2, ph2, sp2, cp2, A_b, B_b = FB
                V_b = ph2
                mk.dma("Y", sb3(pr2), spl(spd["par_sp"], g),
                       reads=[pr2], writes=[pr2])

                for (bb, spn, ci) in ((A_b, "a_sp", 0), (B_b, "b_sp", 1)):
                    cap = cbc_dram.ap()
                    rep = bass.AP(cap.tensor, ci * 128 * D,
                                  [[cap.ap[0][0], 128], [0, NT], [1, D]])
                    mk.dma("G", sb3(bb), rep, reads=[cbc_buf[ci], bb], writes=[bb])
                    mk.dma("G", sb3(bb), spl(spd[spn], g), reads=[bb], writes=[bb],
                           accum=ALU.add)
                mk.dma("Y", sb3(cp2), spl(spd["cs_sp"], g),
                       reads=[cp2], writes=[cp2])
                mk.dma("Y", sb3(sp2), spl(spd["sn_sp"], g),
                       reads=[sp2], writes=[sp2])
                for t in range(NT):
                    col = g * NT + t
                    mk.emit("V", lambda v, t=t, col=col: v.scalar_tensor_tensor(
                        out=A_b.ap()[:, t * D:(t + 1) * D],
                        in0=A_b.ap()[:, t * D:(t + 1) * D],
                        scalar=s_pos.ap()[:, col:col + 1],
                        in1=cp2.ap()[:, t * D:(t + 1) * D],
                        op0=ALU.mult, op1=ALU.mult),
                        reads=[A_b, s_pos, cp2], writes=[A_b])
                    mk.emit("V", lambda v, t=t, col=col: v.scalar_tensor_tensor(
                        out=B_b.ap()[:, t * D:(t + 1) * D],
                        in0=B_b.ap()[:, t * D:(t + 1) * D],
                        scalar=s_pos.ap()[:, col:col + 1],
                        in1=sp2.ap()[:, t * D:(t + 1) * D],
                        op0=ALU.mult, op1=ALU.mult),
                        reads=[B_b, s_pos, sp2], writes=[B_b])
                mk.emit("V", lambda v: v.tensor_tensor(
                    out=A_b.ap(), in0=A_b.ap(), in1=B_b.ap(), op=ALU.add),
                    reads=[A_b, B_b], writes=[A_b])
                mk.dma("Y", sb3(sp2), spl(spd["kc_sp"], g),
                       reads=[sp2], writes=[sp2])
                mk.dma("Y", sb3(cp2), spl(spd["ks_sp"], g),
                       reads=[cp2], writes=[cp2])
                for (bb, spn, ci) in ((B_b, "u_sp", 2), (V_b, "v_sp", 3)):
                    cap = cbc_dram.ap()
                    rep = bass.AP(cap.tensor, ci * 128 * D,
                                  [[cap.ap[0][0], 128], [0, NT], [1, D]])
                    mk.dma("G", sb3(bb), rep, reads=[cbc_buf[ci], bb], writes=[bb])
                    mk.dma("G", sb3(bb), spl(spd[spn], g), reads=[bb], writes=[bb],
                           accum=ALU.add)
                for t in range(NT):
                    col = g * NT + t
                    mk.emit("V", lambda v, t=t, col=col: v.scalar_tensor_tensor(
                        out=B_b.ap()[:, t * D:(t + 1) * D],
                        in0=B_b.ap()[:, t * D:(t + 1) * D],
                        scalar=s_kv.ap()[:, col:col + 1],
                        in1=sp2.ap()[:, t * D:(t + 1) * D],
                        op0=ALU.mult, op1=ALU.mult),
                        reads=[B_b, s_kv, sp2], writes=[B_b])
                    mk.emit("V", lambda v, t=t, col=col: v.scalar_tensor_tensor(
                        out=V_b.ap()[:, t * D:(t + 1) * D],
                        in0=V_b.ap()[:, t * D:(t + 1) * D],
                        scalar=s_kv.ap()[:, col:col + 1],
                        in1=cp2.ap()[:, t * D:(t + 1) * D],
                        op0=ALU.mult, op1=ALU.mult),
                        reads=[V_b, s_kv, cp2], writes=[V_b])
                mk.emit("V", lambda v: v.tensor_tensor(
                    out=B_b.ap(), in0=B_b.ap(), in1=V_b.ap(), op=ALU.add),
                    reads=[B_b, V_b], writes=[B_b])
                mk.emit("V", lambda v: v.tensor_tensor(
                    out=B_b.ap(), in0=B_b.ap(), in1=A_b.ap(), op=ALU.add),
                    reads=[B_b, A_b], writes=[B_b])
                def bnst2(v):
                    for t in range(NT):
                        for hh in range(2):
                            v.bn_stats(out=stats.ap()[:, 12 * t + 6 * hh:
                                                      12 * t + 6 * hh + 6],
                                       in_=B_b.ap()[:, t * D + hh * 512:
                                                    t * D + (hh + 1) * 512])
                    v.drain()
                    last = None
                    for t in range(NT):
                        last = v.bn_aggr(out=mv.ap()[:, 2 * t:2 * t + 2],
                                         in_=stats.ap()[:, 12 * t:12 * t + 12])
                    return last
                mk.emit("V", bnst2, reads=[B_b], writes=[stats, mv])
                mk.emit("S", mkcols_s, reads=[mv, csts], writes=[colsc])
                def mbv(v):
                    for t in range(NT):
                        v.reciprocal(out=colsc.ap()[:, 4 * t:4 * t + 1],
                                     in_=colsc.ap()[:, 4 * t:4 * t + 1])
                    v.drain()
                    last = None
                    for t in range(NT):
                        last = v.tensor_scalar(
                            out=colsc.ap()[:, 4 * t + 1:4 * t + 2],
                            in0=mv.ap()[:, 2 * t:2 * t + 1],
                            scalar1=colsc.ap()[:, 4 * t:4 * t + 1], scalar2=-1.0,
                            op0=ALU.mult, op1=ALU.mult)
                    return last
                mk.emit("V", mbv, reads=[mv, colsc], writes=[colsc])
                def lnact(s):
                    last = None
                    for t in range(NT):
                        last = s.activation(
                            out=ln_bf.ap()[:, t * D:(t + 1) * D],
                            in_=B_b.ap()[:, t * D:(t + 1) * D], func=AF.Identity,
                            scale=colsc.ap()[:, 4 * t:4 * t + 1],
                            bias=colsc.ap()[:, 4 * t + 1:4 * t + 2])
                    return last
                mk.emit("S", lnact, reads=[B_b, colsc], writes=[ln_bf])
                for t in range(NT):
                    for c in range(KC):
                        mk.emit("T", lambda tt, t=t, c=c: tt.transpose(
                            out=ptr.ap(), in_=ln_bf.ap()[:, t * D + c * 128:
                                                         t * D + (c + 1) * 128],
                            identity=ident.ap()),
                            reads=[ln_bf, ident], writes=[ptr])
                        mk.emit("S", lambda s, t=t, c=c: s.copy(
                            out=ln_fm.ap()[:, c * Tg + t * 128:c * Tg + t * 128 + 128],
                            in_=ptr.ap()), reads=[ptr], writes=[ln_fm])
                mk.dma("Y", xn_fm.ap(), xnfm_sp.ap()[g * 128:(g + 1) * 128, :],
                       reads=[xn_fm], writes=[xn_fm])
                for o in range(KC):
                    bank = pf[o % 2]
                    def qmm(tt, o=o, bank=bank):
                        last = None
                        for k in range(KC):
                            last = tt.matmul(
                                out=bank.ap(),
                                lhsT=WqT.ap()[:, k * D + o * 128:k * D + (o + 1) * 128],
                                rhs=xn_fm.ap()[:, k * Tg:(k + 1) * Tg],
                                start=(k == 0), stop=(k == KC - 1))
                        return last
                    mk.emit("T", qmm, reads=[WqT, xn_fm], writes=[bank])
                    mk.emit("S", lambda s, o=o, bank=bank: s.copy(
                        out=q_fm.ap()[:, o * Tg:(o + 1) * Tg], in_=bank.ap()),
                        reads=[bank], writes=[q_fm])
                for t in range(NT):
                    def gmm(tt, t=t):
                        last = None
                        for e in range(KC):
                            last = tt.matmul(
                                out=pb[2].ap()[:, 0:255],
                                lhsT=q_fm.ap()[:, e * Tg + t * 128:e * Tg + t * 128 + 128],
                                rhs=routT.ap()[:, e * 255:(e + 1) * 255],
                                start=(e == 0), stop=(e == KC - 1))
                        return last
                    mk.emit("T", gmm, reads=[q_fm, routT], writes=[pb[2]])
                    mk.emit("S", lambda s, t=t: s.activation(
                        out=gate_f.ap()[:, t * 255:(t + 1) * 255],
                        in_=pb[2].ap()[:, 0:255], func=AF.Sigmoid),
                        reads=[pb[2]], writes=[gate_f])
                def g2build(v):
                    g2v = g2_f.ap().rearrange("p (t n two) -> p t n two", t=NT, two=2)
                    gv = gate_f.ap().rearrange("p (t n) -> p t n", t=NT)
                    v.tensor_scalar(out=g2v[:, :, :, 0:1].squeeze(3), in0=gv,
                                    scalar1=-1.0, scalar2=1.0, op0=ALU.mult,
                                    op1=ALU.add)
                    v.tensor_copy(out=g2v[:, :, :, 1:2].squeeze(3), in_=gv)
                    v.drain()
                    last = v.memset(lp_f.ap(), 1.0)
                    return last
                mk.emit("V", g2build, reads=[gate_f], writes=[g2_f, lp_f])
                def tree(v):
                    last = None
                    pstride = g2_f.ap().ap[0][0]
                    for t in range(NT):
                        base = t * 510
                        for lvl in range(8):
                            n = 1 << lvl
                            half = 256 // (2 * n)
                            fac = bass.AP(
                                g2_f.ap().tensor, base + 2 * (n - 1),
                                [[pstride, 128], [2, n], [1, 2], [0, half]])
                            lpv = lp_f.ap()[:, t * 256:(t + 1) * 256].rearrange(
                                "p (n two h) -> p n two h", n=n, two=2)
                            last = v.tensor_tensor(out=lpv, in0=lpv, in1=fac,
                                                   op=ALU.mult)
                            v.drain()
                    return last
                mk.emit("V", tree, reads=[g2_f, lp_f], writes=[lp_f])
                mk.emit("S", lambda s: s.copy(out=lp_bf.ap(), in_=lp_f.ap()),
                        reads=[lp_f], writes=[lp_bf])
                for t in range(NT):
                    for lc in range(2):
                        mk.emit("T", lambda tt, t=t, lc=lc: tt.transpose(
                            out=ptr.ap(),
                            in_=lp_bf.ap()[:, t * 256 + lc * 128:t * 256 + (lc + 1) * 128],
                            identity=ident.ap()),
                            reads=[lp_bf, ident], writes=[ptr])
                        mk.emit("S", lambda s, t=t, lc=lc: s.copy(
                            out=lpT.ap()[:, lc * Tg + t * 128:lc * Tg + t * 128 + 128],
                            in_=ptr.ap()), reads=[ptr], writes=[lpT])
                for o in range(KC):
                    bank = pf[o % 2]
                    def rmm(tt, o=o, bank=bank):
                        last = None
                        for k in range(2):
                            last = tt.matmul(
                                out=bank.ap(),
                                lhsT=leaf.ap()[:, k * D + o * 128:k * D + (o + 1) * 128],
                                rhs=lpT.ap()[:, k * Tg:(k + 1) * Tg],
                                start=(k == 0), stop=(k == 1))
                        return last
                    mk.emit("T", rmm, reads=[leaf, lpT], writes=[bank])
                    mk.emit("S", lambda s, o=o, bank=bank: s.copy(
                        out=ret_fm.ap()[:, o * Tg:(o + 1) * Tg], in_=bank.ap()),
                        reads=[bank], writes=[ret_fm])
                for t in range(NT):
                    def womm(tt, t=t):
                        last = None
                        for c in range(KC):
                            for h in range(2):
                                last = tt.matmul(
                                    out=pb[h].ap(),
                                    lhsT=ln_fm.ap()[:, c * Tg + t * 128:
                                                    c * Tg + t * 128 + 128],
                                    rhs=Wo.ap()[:, c * D + h * 512:c * D + (h + 1) * 512],
                                    start=(c == 0), stop=False)
                        for h in range(2):
                            tt.matmul(out=pb[h].ap(), lhsT=ones1b.ap(),
                                      rhs=comb(0, slice(h * 512, (h + 1) * 512)),
                                      start=False, stop=False)
                            last = tt.matmul(out=pb[h].ap(), lhsT=ones1b.ap(),
                                             rhs=comb(1, slice(h * 512, (h + 1) * 512)),
                                             start=False, stop=True)
                        return last
                    mk.emit("T", womm, reads=[ln_fm, Wo, ones1b, combs],
                            writes=[pb[0], pb[1]])
                    def ffmm(tt, t=t):
                        last = None
                        for c in range(KC):
                            for h in range(2):
                                last = tt.matmul(
                                    out=pb[2 + h].ap(),
                                    lhsT=ret_fm.ap()[:, c * Tg + t * 128:
                                                     c * Tg + t * 128 + 128],
                                    rhs=WoutT.ap()[:, c * D + h * 512:
                                                   c * D + (h + 1) * 512],
                                    start=(c == 0), stop=(c == KC - 1))
                        return last
                    mk.emit("T", ffmm, reads=[ret_fm, WoutT], writes=[pb[2], pb[3]])
                    def asm(v, t=t):
                        for h in range(2):
                            v.tensor_tensor(
                                out=A_b.ap()[:, t * D + h * 512:t * D + (h + 1) * 512],
                                in0=pr2.ap()[:, t * D + h * 512:t * D + (h + 1) * 512],
                                in1=pb[h].ap(), op=ALU.add)
                        v.drain()
                        last = None
                        for h in range(2):
                            last = v.tensor_tensor(
                                out=A_b.ap()[:, t * D + h * 512:t * D + (h + 1) * 512],
                                in0=A_b.ap()[:, t * D + h * 512:t * D + (h + 1) * 512],
                                in1=pb[2 + h].ap(), op=ALU.add)
                        return last
                    mk.emit("V", asm, reads=[pr2, pb[0], pb[1], pb[2], pb[3], A_b],
                            writes=[A_b])
                mk.dma("Y", y.ap()[g * Tg:(g + 1) * Tg, :]
                       .rearrange("(t p) f -> p t f", p=128), sb3(A_b),
                       reads=[A_b])

        final = [(f"{q}Q{i}", mk.ctr[f"{q}Q{i}"])
                 for q in "YG" for i in range(MK.NQSEM)
                 if mk.ctr[f"{q}Q{i}"] > 0]
        mk.finalize(final)
        if mode in ("fused", "back"):
            wstack2.close()
        elif mode == "front":
            wstack.close()
    return mk


def _host_inputs(cfg, inputs, n_cores):
    B, L, cps, Tg = cfg["B"], cfg["L"], cfg["cps"], cfg["Tg"]
    Lc = L // cps
    x = np.ascontiguousarray(np.asarray(inputs["L0_attn_x"], np.float32))
    ph = np.ascontiguousarray(np.asarray(inputs["L0_attn_pos_phases"],
                                         np.float32))[:L]
    wmap = {
        "wpv": "L0_attn_Wpv", "bpv": "L0_attn_bpv", "wk": "L0_attn_Wk",
        "bk": "L0_attn_bk", "wkv": "L0_attn_Wkv", "bkv": "L0_attn_bkv",
        "wg1": "L0_attn_Wg1", "bg1": "L0_attn_bg1", "wg2": "L0_attn_Wg2",
        "bg2": "L0_attn_bg2", "wb1": "L0_attn_Wb1", "bb1": "L0_attn_bb1",
        "wb2": "L0_attn_Wb2", "bb2": "L0_attn_bb2", "lng": "L0_attn_ln_g",
        "lnb": "L0_attn_ln_b", "wo": "L0_attn_Wo", "bo": "L0_attn_bo",
        "wq": "L0_ffn_Wq", "rout": "L0_ffn_routers", "lfv": "L0_ffn_leaf_values",
        "wout": "L0_ffn_Wout_w", "woutb": "L0_ffn_Wout_b",
    }
    shared = {k: np.ascontiguousarray(np.asarray(inputs[v], np.float32))
              for k, v in wmap.items()}
    import ml_dtypes
    tri = np.triu(np.ones((128, 128), np.float32))
    ident = np.eye(128, dtype=np.float32).astype(ml_dtypes.bfloat16)
    ones1f = np.ones((1, 128), np.float32)
    ones1b = np.ones((1, 128), np.float32).astype(ml_dtypes.bfloat16)
    consts = np.zeros((128, 8), np.float32)
    consts[:, 0] = 1.0
    consts[:, 1] = -PI
    consts[:, 2] = PI / 2
    consts[:, 3] = 1e-5
    consts[:, 4] = PI
    consts[:, 5] = 0.0
    consts[:, 6] = -1.0
    consts[:, 7] = 64 * PI
    in_maps = []
    for c in range(n_cores):
        b, k = c // cps, c % cps
        rows = slice(k * Lc, (k + 1) * Lc)
        m = dict(shared)
        m["x"] = np.ascontiguousarray(x[b, rows])
        m["ph"] = np.ascontiguousarray(ph[rows])
        m["tri"] = tri
        m["ident_hbm"] = ident
        m["ones1f_hbm"] = ones1f
        m["ones1b_hbm"] = ones1b
        m["consts_hbm"] = consts
        cmask = np.zeros((5, cps), np.float32)
        cmask[:, :k] = 1.0
        m["cmask_hbm"] = cmask
        if k > 0:
            xp = x[b, k * Lc - 1]
            m["bflag"] = np.ones((1, 1), np.float32)
        else:
            xp = np.zeros(D, np.float32)
            m["bflag"] = np.zeros((1, 1), np.float32)
        m["xprev"] = np.ascontiguousarray(xp.reshape(1, D))
        m["xprev_fm"] = np.ascontiguousarray(xp.reshape(KC, 128).T)
        in_maps.append(m)
    return in_maps


MODE = "split"

_SP_NAMES = ("a_sp", "b_sp", "u_sp", "v_sp", "kc_sp", "ks_sp", "par_sp",
             "cs_sp", "sn_sp", "xnfm_sp")


def kernel(**inputs):
    cfg = FULL_CFG
    n_cores = cfg["B"] * cfg["cps"]
    cps = cfg["cps"]
    in_maps = _host_inputs(cfg, inputs, n_cores)
    Lc = cfg["L"] // cps
    if MODE == "fused":
        mk = build(cfg, "fused")
        res = run_bass_kernel_spmd(mk.nc, in_maps,
                                   core_ids=list(range(n_cores)))
        results = res.results
    else:
        mkf = build(cfg, "front")
        r1 = run_bass_kernel_spmd(mkf.nc, in_maps,
                                  core_ids=list(range(n_cores))).results
        in2 = []
        for c in range(n_cores):
            b = c // cps
            m = dict(in_maps[c])
            m["tots"] = np.ascontiguousarray(np.concatenate(
                [np.asarray(r1[b * cps + r]["tot"]) for r in range(cps)],
                axis=0))
            m["ga_in"] = np.asarray(r1[c]["ga_out"])
            m["da_in"] = np.asarray(r1[c]["da_out"])
            for nsp in _SP_NAMES:
                m[nsp] = np.asarray(r1[c][nsp])
            in2.append(m)
        mkb = build(cfg, "back")
        results = run_bass_kernel_spmd(mkb.nc, in2,
                                       core_ids=list(range(n_cores))).results
    out = np.empty((cfg["B"], cfg["L"], D), np.float32)
    for c in range(n_cores):
        b, k = c // cps, c % cps
        out[b, k * Lc:(k + 1) * Lc] = results[c]["y"]
    return out
